# version 05
# baseline (speedup 1.0000x reference)
import numpy as np
from contextlib import ExitStack
import concourse.bass as bass
import concourse.mybir as mybir
from concourse.bass_utils import run_bass_kernel_spmd

F32 = mybir.dt.float32
BF16 = mybir.dt.bfloat16
I8 = mybir.dt.int8
AF = mybir.ActivationFunctionType
ALU = mybir.AluOpType
AX = mybir.AxisListType

D = 1024
KC = 8
H = 16
DH = 64
TOWN = 2048
TFULL = 4096
EXPC = 12.0
LNEPS = 1e-5


class Ctr:
    def __init__(self, ctx, nc, name):
        self.h = ctx.enter_context(nc.semaphore(name))
        self.n = 0

    def inc(self, instr, k=1):
        instr.then_inc(self.h, k)
        self.n += k
        return self.n


class Bld:

    def __init__(self, nc, ctx, block):
        self.nc, self.ctx, self.block = nc, ctx, block

    def sb(self, stack, name, shape, dtype):
        return stack.enter_context(self.nc.sbuf_tensor(name, shape, dtype))

    def ps(self, stack, name, shape, dtype=F32):
        return stack.enter_context(self.nc.psum_tensor(name, shape, dtype))

    def ctr(self, name):
        return Ctr(self.ctx, self.nc, name)


ALL_ENGINES = ("sync", "tensor", "vector", "scalar", "gpsimd")


def barrier(b, bar, producers):
    for eng in producers:
        def mk(e):
            e.drain()
            bar.n += 0
            e.sem_inc(bar.h, 1)
        getattr(b.block, eng)(mk)
        bar.n += 1
    tgt = bar.n
    for eng in ALL_ENGINES:
        def mk2(e, t=tgt):
            e.wait_ge(bar.h, t)
        getattr(b.block, eng)(mk2)



def emit_load_cast_x(b, st, xT, xbf, s_x2, s_cast, ncols, also_square=None):
    xstg = [b.sb(st, f"xstg{i}", [128, ncols], F32) for i in range(2)]
    cast_base = s_cast.n
    x_base = [c.n for c in s_x2]

    def fsync(sync):
        for k in range(KC):
            if k >= 2:
                sync.wait_ge(s_cast.h, cast_base + (k - 1))
            ins = sync.dma_start(out=xstg[k % 2][:], in_=xT[k * 128:(k + 1) * 128, 0:ncols])
            s_x2[k % 2].inc(ins, 16)
    b.block.sync(fsync)

    def fscalar(scalar):
        for k in range(KC):
            scalar.wait_ge(s_x2[k % 2].h, x_base[k % 2] + 16 * (k // 2 + 1))
            ins = scalar.copy(xbf[k][:, 0:ncols], xstg[k % 2][:])
            if also_square is not None:
                ins = scalar.activation(also_square[k][:, 0:ncols], xstg[k % 2][:], AF.Square)
            s_cast.inc(ins, 1)
    b.block.scalar(fscalar)


def emit_ln(b, st, xbf, xsq, ncols, tiles):
    s_pe = b.ctr("ln_pe")
    s_se = b.ctr("ln_se")
    s_ve = b.ctr("ln_ve")
    s_gp = b.ctr("ln_gp")
    ones, mu_bc, musq, var_bc, r_bc = (tiles[k] for k in ("ones", "mu_bc", "musq", "var_bc", "r_bc"))
    tmp_sub = tiles["tmp_sub"]
    pmu, psq = tiles["pmu"], tiles["psq"]
    NT = ncols // 512

    def ftensor(tensor):
        for t in range(NT):
            sl = slice(t * 512, (t + 1) * 512)
            if t >= 1:
                tensor.wait_ge(s_se.h, 2 * t - 1)
                tensor.wait_ge(s_ve.h, 10 * t - 9)
            for k in range(KC):
                tensor.matmul(pmu[:], ones[:], xbf[k][:, sl], start=(k == 0), stop=(k == KC - 1))
            for k in range(KC):
                ins = tensor.matmul(psq[:], ones[:], xsq[k][:, sl], start=(k == 0), stop=(k == KC - 1))
            s_pe.inc(ins, 1)
    b.block.tensor(ftensor)

    def fscalar(scalar):
        for t in range(NT):
            scalar.wait_ge(s_pe.h, t + 1)
            if t >= 1:
                scalar.wait_ge(s_ve.h, 10 * t)
            ins = scalar.activation(mu_bc[:], pmu[:], AF.Copy, scale=1.0 / D)
            s_se.inc(ins, 1)
            scalar.wait_ge(s_ve.h, 10 * t + 1)
            ins = scalar.activation(musq[:], var_bc[:], AF.Sqrt, bias=tiles["eps_t"][:])
            s_se.inc(ins, 1)
    b.block.scalar(fscalar)

    def fvector(vector):
        for t in range(NT):
            vector.wait_ge(s_se.h, 2 * t + 1)
            vector.tensor_tensor(musq[:], mu_bc[:], mu_bc[:], ALU.mult)
            vector.drain()
            ins = vector.scalar_tensor_tensor(var_bc[:], psq[:], 1.0 / D, musq[:], ALU.mult, ALU.subtract)
            s_ve.inc(ins, 1)
            vector.wait_ge(s_se.h, 2 * t + 2)
            if t >= 1:
                vector.wait_ge(s_gp.h, 8 * t)
            ins = vector.reciprocal(r_bc[:], musq[:])
            s_ve.inc(ins, 1)
            for k in range(KC):
                sl = slice(t * 512, (t + 1) * 512)
                if t * KC + k >= 2:
                    vector.wait_ge(s_gp.h, t * KC + k - 1)
                ins = vector.tensor_tensor(tmp_sub[k % 2][:], xbf[k][:, sl], mu_bc[:], ALU.subtract)
                s_ve.inc(ins, 1)
    b.block.vector(fvector)

    def fgp(g):
        for t in range(NT):
            for k in range(KC):
                sl = slice(t * 512, (t + 1) * 512)
                g.wait_ge(s_ve.h, 10 * t + 2 + k + 1)
                ins = g.tensor_tensor(xbf[k][:, sl], tmp_sub[k % 2][:], r_bc[:], ALU.mult)
                s_gp.inc(ins, 1)
    b.block.gpsimd(fgp)



def build_attn_layer():
    nc = bass.Bass("TRN2", target_bir_lowering=False)
    xT = nc.dram_tensor("xT", [D, TFULL], F32, kind="ExternalInput").ap()
    wqT = nc.dram_tensor("wqT", [D, D], F32, kind="ExternalInput").ap()
    wkT = nc.dram_tensor("wkT", [D, D], F32, kind="ExternalInput").ap()
    wvT = nc.dram_tensor("wvT", [D, D], F32, kind="ExternalInput").ap()
    bq = nc.dram_tensor("bq", [128, KC], F32, kind="ExternalInput").ap()
    bk = nc.dram_tensor("bk", [128, KC], F32, kind="ExternalInput").ap()
    bvrow = nc.dram_tensor("bvrow", [1, D], F32, kind="ExternalInput").ap()
    woT = nc.dram_tensor("woT", [D, D], F32, kind="ExternalInput").ap()
    bo = nc.dram_tensor("bo", [128, KC], F32, kind="ExternalInput").ap()
    rout = nc.dram_tensor("rout", [D, TOWN], F32, kind="ExternalOutput").ap()
    aosc = nc.dram_tensor("aosc", [D, TOWN], BF16, kind="Internal").ap()

    with ExitStack() as ctx:
        blk = ctx.enter_context(nc.Block())
        b = Bld(nc, ctx, blk)
        emit_attn_layer(b, ctx, xT, wqT, wkT, wvT, bq, bk, bvrow, woT, bo, rout, aosc)
    return nc


def emit_attn_layer(b, root, xT, wqT, wkT, wvT, bq, bk, bvrow, woT, bo, rout, aosc):
    nc = b.nc
    bar = b.ctr("bar")

    ones = b.sb(root, "ones", [128, 128], BF16)
    onesf = b.sb(root, "onesf", [1, 128], F32)
    biasq = b.sb(root, "biasq", [128, KC], F32)
    biask = b.sb(root, "biask", [128, KC], F32)
    biaso = b.sb(root, "biaso", [128, KC], F32)
    bvr = b.sb(root, "bvr", [1, D], BF16)
    bvrf = b.sb(root, "bvrf", [1, D], F32)
    eps_t = b.sb(root, "eps_t", [128, 1], F32)
    ident128 = b.sb(root, "ident128", [128, 128], F32)
    s_misc = b.ctr("s_misc")

    def fsync0(sync):
        ins = sync.dma_start(out=biasq[:], in_=bq)
        s_misc.inc(ins, 16)
        ins = sync.dma_start(out=biask[:], in_=bk)
        s_misc.inc(ins, 16)
        ins = sync.dma_start(out=biaso[:], in_=bo)
        s_misc.inc(ins, 16)
        ins = sync.dma_start(out=bvrf[:], in_=bvrow)
        s_misc.inc(ins, 16)
    b.block.sync(fsync0)

    def fgp0(g):
        g.memset(ones[:], 1.0)
        g.memset(onesf[:], 1.0)
        g.memset(eps_t[:], LNEPS)
        g.memset(ident128[:], 0.0)
        g.drain()
        g.affine_select(out=ident128[:], in_=ident128[:], compare_op=ALU.not_equal,
                        fill=1.0, base=0, pattern=[[-1, 128]], channel_multiplier=1)
        g.wait_ge(s_misc.h, 64)
        g.tensor_copy(bvr[:], bvrf[:])
    b.block.gpsimd(fgp0)

    with ExitStack() as stA:
        xbf = [b.sb(root, f"xbf{k}", [128, TFULL], BF16) for k in range(KC)]
        xsq = [b.sb(stA, f"xsq{k}", [128, TFULL], BF16) for k in range(KC)]
        s_x2 = [b.ctr("s_x0"), b.ctr("s_x1")]
        s_cast = b.ctr("s_cast")
        emit_load_cast_x(b, stA, xT, xbf, s_x2, s_cast, TFULL, also_square=xsq)
        barrier(b, bar, ("sync", "scalar", "gpsimd"))

        ln = {
            "ones": ones,
            "eps_t": eps_t,
            "mu_bc": b.sb(stA, "mu_bc", [128, 512], F32),
            "musq": b.sb(stA, "musq", [128, 512], F32),
            "var_bc": b.sb(stA, "var_bc", [128, 512], F32),
            "r_bc": b.sb(stA, "r_bc", [128, 512], F32),
            "tmp_sub": [b.sb(stA, f"tmp_sub{i}", [128, 512], F32) for i in range(2)],
            "pmu": b.ps(stA, "pmu", [128, 512]),
            "psq": b.ps(stA, "psq", [128, 512]),
        }
        emit_ln(b, stA, xbf, xsq, TFULL, ln)
        barrier(b, bar, ("tensor", "scalar", "vector", "gpsimd"))

    with ExitStack() as stB:
        wstg = [b.sb(stB, f"wstg{i}", [128, 256], F32) for i in range(2)]
        wq_g = [b.sb(stB, f"wq_g{k}", [128, 256], BF16) for k in range(KC)]
        wk_g = [b.sb(stB, f"wk_g{k}", [128, 256], BF16) for k in range(KC)]
        wv_g = [b.sb(stB, f"wv_g{k}", [128, 256], BF16) for k in range(KC)]
        kgh = [b.sb(stB, f"kgh{i}", [65, TFULL], BF16) for i in range(4)]
        qgh = [b.sb(stB, f"qgh{i}", [65, TOWN], BF16) for i in range(4)]
        vg = b.sb(stB, "vg", [128, 32 * 260], BF16)
        pT = b.sb(stB, "pT", [128, 32 * 512], BF16)
        den_row = b.sb(stB, "den_row", [1, 512], F32)
        den_rec = b.sb(stB, "den_rec", [1, 512], F32)
        den_bc = b.sb(stB, "den_bc", [64, 512], F32)
        mtmp8 = b.sb(stB, "mtmp8", [128, KC], F32)
        m_sb = b.sb(stB, "m_sb", [128, 1], F32)
        ao_g = [b.sb(stB, f"ao_g{i}", [128, TOWN], BF16) for i in range(2)]
        pk = [b.ps(stB, f"pk{i}", [128, 512]) for i in range(2)]
        psc = [b.ps(stB, f"psc{i}", [128, 512]) for i in range(2)]
        psA = [b.ps(stB, f"psA{i}", [128, 512]) for i in range(2)]
        pv = b.ps(stB, "pv", [128, 512])
        pbc = b.ps(stB, "pbc", [128, 512])

        for g in range(4):
            emit_attn_group(b, g, bar, wstg, wq_g, wk_g, wv_g, kgh, qgh, vg, pT,
                            den_row, den_rec, den_bc, mtmp8, m_sb, ao_g, pk, psc, psA, pv, pbc,
                            ones, onesf, ident128, bvr, biasq, biask, xbf,
                            wqT, wkT, wvT, aosc)

    with ExitStack() as stE:
        wstg2 = b.sb(stE, "wstg2", [128, 1024], F32)
        wo_c = [b.sb(stE, f"wo_c{k}", [128, 1024], BF16) for k in range(KC)]
        ao_c = [b.sb(stE, f"ao_c{k}", [128, TOWN], BF16) for k in range(KC)]
        xres = [b.sb(stE, f"xres{i}", [128, TOWN], F32) for i in range(2)]
        out_c = [b.sb(stE, f"out_c{i}", [128, TOWN], F32) for i in range(2)]
        ppr = [b.ps(stE, f"ppr{i}", [128, 512]) for i in range(2)]
        emit_proj(b, bar, wstg2, wo_c, ao_c, xres, out_c, ppr, biaso, woT, xT, aosc, rout)


def emit_attn_group(b, g, bar, wstg, wq_g, wk_g, wv_g, kgh, qgh, vg, pT,
                    den_row, den_rec, den_bc, mtmp8, m_sb, ao_g, pk, psc, psA, pv, pbc,
                    ones, onesf, ident128, bvr, biasq, biask, xbf,
                    wqT, wkT, wvT, aosc):
    nc = b.nc
    fs = slice(g * 256, (g + 1) * 256)

    s_wd = b.ctr(f"g{g}_wd")
    s_wc = b.ctr(f"g{g}_wc")

    def fgp(gp):
        for k in range(KC):
            for j, (wt, dst) in enumerate(((wqT, wq_g), (wkT, wk_g), (wvT, wv_g))):
                i = 3 * k + j
                if i >= 2:
                    gp.wait_ge(s_wc.h, i - 1)
                ins = gp.dma_start(out=wstg[i % 2][:], in_=wt[k * 128:(k + 1) * 128, fs])
                s_wd.inc(ins, 16)
                gp.wait_ge(s_wd.h, s_wd.n)
                ins = gp.tensor_copy(dst[k][:], wstg[i % 2][:])
                s_wc.inc(ins, 1)
    b.block.gpsimd(fgp)
    barrier(b, bar, ("gpsimd",))

    def fgp2(gp):
        view = vg[:].rearrange("p (n c) -> p n c", c=65)
        gp.memset(view[:, :, 64:65], 1.0)
        for i in range(4):
            gp.memset(kgh[i][64:65, :], 1.0)
    b.block.gpsimd(fgp2)

    s_pe = b.ctr(f"g{g}_pe")
    s_ev = b.ctr(f"g{g}_ev")

    def ftensor(tensor):
        nmm = 0
        for mc in range(2):
            for t in range(TFULL // 512):
                nmm += 1
                if nmm > 2:
                    tensor.wait_ge(s_ev.h, nmm - 2)
                for k in range(KC):
                    ins = tensor.matmul(pk[nmm % 2][:], wk_g[k][:, mc * 128:(mc + 1) * 128],
                                        xbf[k][:, t * 512:(t + 1) * 512],
                                        start=(k == 0), stop=(k == KC - 1))
                s_pe.inc(ins, 1)
        for mc in range(2):
            for t in range(TOWN // 512):
                nmm += 1
                if nmm > 2:
                    tensor.wait_ge(s_ev.h, nmm - 2)
                for k in range(KC):
                    ins = tensor.matmul(pk[nmm % 2][:], wq_g[k][:, mc * 128:(mc + 1) * 128],
                                        xbf[k][:, t * 512:(t + 1) * 512],
                                        start=(k == 0), stop=(k == KC - 1))
                s_pe.inc(ins, 1)
        for mt in range(32):
            nmm += 1
            if nmm > 2:
                tensor.wait_ge(s_ev.h, nmm - 2)
            for k in range(KC):
                tensor.matmul(pk[nmm % 2][:, 0:256], xbf[k][:, mt * 128:(mt + 1) * 128],
                              wv_g[k][:], start=(k == 0), stop=False)
            ins = tensor.matmul(pk[nmm % 2][:, 0:256], ones[0:1, :],
                                bvr[0:1, fs], start=False, stop=True)
            s_pe.inc(ins, 1)
    b.block.tensor(ftensor)

    def fvector(vector):
        nmm = 0
        for wsel, nt, dst_own in ((0, TFULL // 512, False), (1, TOWN // 512, True)):
            for mc in range(2):
                for t in range(nt):
                    nmm += 1
                    vector.wait_ge(s_pe.h, nmm)
                    bias = biask if wsel == 0 else biasq
                    col = 2 * g + mc
                    for j in range(2):
                        dst = (qgh if wsel else kgh)[2 * mc + j]
                        ins = vector.tensor_scalar(
                            dst[0:64, t * 512:(t + 1) * 512],
                            pk[nmm % 2][j * 64:(j + 1) * 64, :],
                            bias[j * 64:(j + 1) * 64, col:col + 1], None, ALU.add)
                    s_ev.inc(ins, 1)
        for mt in range(32):
            nmm += 1
            vector.wait_ge(s_pe.h, nmm)
            dst = vg[:, mt * 260:(mt + 1) * 260].rearrange("p (n c) -> p n c", c=65)[:, :, 0:64]
            src = pk[nmm % 2][:, 0:256].rearrange("p (n c) -> p n c", c=64)
            ins = vector.tensor_copy(dst, src)
            s_ev.inc(ins, 1)
    b.block.vector(fvector)
    barrier(b, bar, ("tensor", "vector", "gpsimd"))

    s_m1 = b.ctr(f"g{g}_m1")
    s_m1v = b.ctr(f"g{g}_m1v")
    s_m1t = b.ctr(f"g{g}_m1t")
    s_m1s = b.ctr(f"g{g}_m1s")
    s_sc = b.ctr(f"g{g}_sc")
    s_ex = b.ctr(f"g{g}_ex")
    s_av = b.ctr(f"g{g}_av")
    s_dr = b.ctr(f"g{g}_dr")
    s_rc = b.ctr(f"g{g}_rc")
    s_db = b.ctr(f"g{g}_db")
    s_no = b.ctr(f"g{g}_no")

    NQC = TOWN // 128

    def ftensor2(tensor):
        it = 0
        nm1 = 0
        for hp in range(4):
            for qc in range(NQC):
                qq = hp * NQC + qc
                for kc in range(8):
                    nm1 += 1
                    if qq * 8 + kc >= 2:
                        tensor.wait_ge(s_m1v.h, 9 * ((qq * 8 + kc - 2) // 8) + ((qq * 8 + kc - 2) % 8) + 1)
                    ins = tensor.matmul(psA[kc % 2][:],
                                        qgh[hp][0:64, qc * 128:(qc + 1) * 128],
                                        kgh[hp][0:64, kc * 512:(kc + 1) * 512],
                                        start=True, stop=True)
                    s_m1.inc(ins, 1)
                tensor.wait_ge(s_m1v.h, 9 * (qq + 1))
                if hp > 0 and qc == 0:
                    tensor.wait_ge(s_db.h, 4 * hp)
                if qq >= 1:
                    tensor.wait_ge(s_m1s.h, qq)
                ins = tensor.transpose(pbc[0:1, 0:128], m_sb[:], ident128[:])
                s_m1t.inc(ins, 1)
            tensor.wait_ge(s_m1s.h, NQC * (hp + 1))
            for qt in range(4):
                it += 1
                qsl = slice(qt * 512, (qt + 1) * 512)
                if it > 1:
                    tensor.wait_ge(s_no.h, it - 1)
                for kc in range(32):
                    if kc >= 2:
                        tensor.wait_ge(s_ex.h, (it - 1) * 32 + kc - 1)
                    ins = tensor.matmul(psc[kc % 2][:], kgh[hp][0:65, kc * 128:(kc + 1) * 128],
                                        qgh[hp][0:65, qsl], start=True, stop=True)
                    s_sc.inc(ins, 1)
                tensor.wait_ge(s_ex.h, it * 32)
                for kc in range(32):
                    ins = tensor.matmul(pv[0:65, :], vg[:, kc * 260 + hp * 65: kc * 260 + (hp + 1) * 65],
                                        pT[:, kc * 512:(kc + 1) * 512],
                                        start=(kc == 0), stop=(kc == 31))
                s_av.inc(ins, 1)
                tensor.wait_ge(s_rc.h, it)
                if it > 1:
                    tensor.wait_ge(s_db.h, it - 1)
                ins = tensor.matmul(pbc[0:64, :], onesf[0:1, 0:64], den_rec[:], start=True, stop=True)
                s_sc.inc(ins, 1)
    b.block.tensor(ftensor2)

    def fvector2(vector):
        it = 0
        nm1 = 0
        for hp in range(4):
            for qc in range(NQC):
                qq = hp * NQC + qc
                for kc in range(8):
                    nm1 += 1
                    vector.wait_ge(s_m1.h, nm1)
                    ins = vector.reduce_max(mtmp8[:, kc:kc + 1], psA[kc % 2][:], axis=AX.X)
                    s_m1v.inc(ins, 1)
                vector.drain()
                if qq >= 1:
                    vector.wait_ge(s_m1t.h, qq)
                ins = vector.reduce_max(m_sb[:], mtmp8[:], axis=AX.X)
                s_m1v.inc(ins, 1)
            for qt in range(4):
                it += 1
                qsl = slice(qt * 512, (qt + 1) * 512)
                mc, pb = hp // 2, (hp % 2) * 64
                vector.wait_ge(s_dr.h, it)
                ins = vector.reciprocal(den_rec[:], den_row[:])
                s_rc.inc(ins, 1)
                vector.wait_ge(s_db.h, it)
                ins = vector.tensor_tensor(ao_g[mc][pb:pb + 64, qsl], pv[0:64, :],
                                           den_bc[:], ALU.mult)
                s_no.inc(ins, 1)
    b.block.vector(fvector2)

    def fscalar2(scalar):
        it = 0
        nsc = 0
        for hp in range(4):
            for qc in range(NQC):
                qq = hp * NQC + qc
                scalar.wait_ge(s_m1t.h, qq + 1)
                ins = scalar.activation(qgh[hp][64:65, qc * 128:(qc + 1) * 128],
                                        pbc[0:1, 0:128], AF.Copy, scale=-1.0)
                s_m1s.inc(ins, 1)
            for qt in range(4):
                it += 1
                if it > 1:
                    scalar.wait_ge(s_av.h, it - 1)
                for kc in range(32):
                    nsc += 1
                    scalar.wait_ge(s_sc.h, nsc + (it - 1))
                    ins = scalar.activation(pT[:, kc * 512:(kc + 1) * 512], psc[kc % 2][:],
                                            AF.Exp, scale=0.125)
                    s_ex.inc(ins, 1)
                scalar.wait_ge(s_av.h, it)
                ins = scalar.copy(den_row[:], pv[64:65, :])
                s_dr.inc(ins, 1)
                scalar.wait_ge(s_sc.h, nsc + it)
                ins = scalar.copy(den_bc[:], pbc[0:64, :])
                s_db.inc(ins, 1)
    b.block.scalar(fscalar2)
    barrier(b, bar, ("tensor", "scalar", "vector"))

    s_ao = b.ctr(f"g{g}_ao")

    def fsync(sync):
        for mc in range(2):
            ins = sync.dma_start(out=aosc[g * 256 + mc * 128: g * 256 + (mc + 1) * 128, :],
                                 in_=ao_g[mc][:])
            s_ao.inc(ins, 16)
        sync.wait_ge(s_ao.h, s_ao.n)
    b.block.sync(fsync)
    barrier(b, bar, ("sync",))


def emit_proj(b, bar, wstg2, wo_c, ao_c, xres, out_c, ppr, biaso, woT, xT, aosc, rout):
    nc = b.nc
    s_wd = b.ctr("pj_wd")
    s_wc = b.ctr("pj_wc")
    s_ad = b.ctr("pj_ad")
    s_xd = [b.ctr("pj_xd0"), b.ctr("pj_xd1")]
    s_pe = b.ctr("pj_pe")
    s_ev = b.ctr("pj_ev")
    s_st = [b.ctr("pj_st0"), b.ctr("pj_st1")]

    def fgp(gp):
        for k in range(KC):
            if k >= 1:
                gp.wait_ge(s_wc.h, k)
            ins = gp.dma_start(out=wstg2[:], in_=woT[k * 128:(k + 1) * 128, :])
            s_wd.inc(ins, 16)
            gp.wait_ge(s_wd.h, s_wd.n)
            ins = gp.tensor_copy(wo_c[k][:], wstg2[:])
            s_wc.inc(ins, 1)
    b.block.gpsimd(fgp)

    def fsync(sync):
        for k in range(KC):
            ins = sync.dma_start(out=ao_c[k][:], in_=aosc[k * 128:(k + 1) * 128, :])
            s_ad.inc(ins, 16)
        for mc in range(KC):
            if mc >= 2:
                sync.wait_ge(s_ev.h, (mc - 1) * 4)
            ins = sync.dma_start(out=xres[mc % 2][:], in_=xT[mc * 128:(mc + 1) * 128, 0:TOWN])
            s_xd[mc % 2].inc(ins, 16)
            if mc >= 1:
                sync.wait_ge(s_ev.h, mc * 4)
                ins = sync.dma_start(out=rout[(mc - 1) * 128:mc * 128, :], in_=out_c[(mc - 1) % 2][:])
                s_st[(mc - 1) % 2].inc(ins, 16)
        sync.wait_ge(s_ev.h, KC * 4)
        ins = sync.dma_start(out=rout[(KC - 1) * 128:KC * 128, :], in_=out_c[(KC - 1) % 2][:])
        s_st[(KC - 1) % 2].inc(ins, 16)
        sync.wait_ge(s_st[0].h, s_st[0].n)
        sync.wait_ge(s_st[1].h, s_st[1].n)
    b.block.sync(fsync)

    def ftensor(tensor):
        tensor.wait_ge(s_ad.h, 16 * KC)
        tensor.wait_ge(s_wc.h, KC)
        it = 0
        for mc in range(KC):
            for t in range(TOWN // 512):
                it += 1
                if it > 2:
                    tensor.wait_ge(s_ev.h, it - 2)
                for k in range(KC):
                    ins = tensor.matmul(ppr[it % 2][:], wo_c[k][:, mc * 128:(mc + 1) * 128],
                                        ao_c[k][:, t * 512:(t + 1) * 512],
                                        start=(k == 0), stop=(k == KC - 1))
                s_pe.inc(ins, 1)
    b.block.tensor(ftensor)

    def fvector(vector):
        it = 0
        for mc in range(KC):
            for t in range(TOWN // 512):
                it += 1
                vector.wait_ge(s_pe.h, it)
                vector.wait_ge(s_xd[mc % 2].h, 16 * (mc // 2 + 1))
                if mc >= 2 and t == 0:
                    vector.wait_ge(s_st[mc % 2].h, 16 * ((mc - 2) // 2 + 1))
                ins = vector.scalar_tensor_tensor(out_c[mc % 2][:, t * 512:(t + 1) * 512],
                                                  ppr[it % 2][:], biaso[:, mc:mc + 1],
                                                  xres[mc % 2][:, t * 512:(t + 1) * 512],
                                                  ALU.add, ALU.add)
                s_ev.inc(ins, 1)
    b.block.vector(fvector)
    barrier(b, bar, ("sync", "tensor", "vector", "gpsimd"))



def _canon_qkv_l0(w_qkv):
    return w_qkv[0:D], w_qkv[D:2 * D], w_qkv[2 * D:3 * D]


def _canon_qkv_l1(w_qkv):
    idx_q = np.concatenate([np.arange(h * 192, h * 192 + 64) for h in range(H)])
    idx_k = idx_q + 64
    idx_v = idx_q + 128
    return w_qkv[idx_q], w_qkv[idx_k], w_qkv[idx_v]


def _bias_cols(v):
    return np.ascontiguousarray(v.reshape(KC, 128).T)


def _attn_inputs(x_full, wq, wk, wv, bqv, bkv, bvv, wo, bov):
    maps = []
    for c in range(8):
        bidx, half = c // 2, c % 2
        xb = x_full[bidx]
        if half == 1:
            xb = np.concatenate([xb[TOWN:], xb[:TOWN]], axis=0)
        maps.append({
            "xT": np.ascontiguousarray(xb.T),
            "wqT": np.ascontiguousarray(wq.T),
            "wkT": np.ascontiguousarray(wk.T),
            "wvT": np.ascontiguousarray(wv.T),
            "bq": _bias_cols(bqv),
            "bk": _bias_cols(bkv),
            "bvrow": np.ascontiguousarray(bvv.reshape(1, D)),
            "woT": np.ascontiguousarray(wo.T),
            "bo": _bias_cols(bov),
        })
    return maps


def _attn_collect(res):
    out = np.empty((4, TFULL, D), dtype=np.float32)
    for c in range(8):
        bidx, half = c // 2, c % 2
        r = np.asarray(res.results[c]["rout"])
        out[bidx, half * TOWN:(half + 1) * TOWN] = r.T
    return out





def build_moe_layer():
    nc = bass.Bass("TRN2", target_bir_lowering=False)
    rin = nc.dram_tensor("rin", [D, TOWN], F32, kind="ExternalInput").ap()
    wg = nc.dram_tensor("wg", [D, 8], F32, kind="ExternalInput").ap()
    w1 = nc.dram_tensor("w1", [8, D, 2048], F32, kind="ExternalInput").ap()
    w2 = nc.dram_tensor("w2", [8, 2048, D], F32, kind="ExternalInput").ap()
    rout = nc.dram_tensor("rout", [D, TOWN], F32, kind="ExternalOutput").ap()
    with ExitStack() as ctx:
        blk = ctx.enter_context(nc.Block())
        b = Bld(nc, ctx, blk)
        emit_moe_layer(b, ctx, rin, wg, w1, w2, rout)
    return nc


def emit_moe_layer(b, root, rin, wg, w1, w2, rout):
    nc = b.nc
    bar = b.ctr("bar")
    NE = 8
    HID = 2048
    NMT = TOWN // 128
    NTT = TOWN // 512

    ones = b.sb(root, "ones", [128, 128], BF16)
    onesf = b.sb(root, "onesf", [1, 128], F32)
    eps_t = b.sb(root, "eps_t", [128, 1], F32)
    ident128 = b.sb(root, "ident128", [128, 128], F32)
    xbf = [b.sb(root, f"xbf{k}", [128, TOWN], BF16) for k in range(KC)]
    y_acc = [b.sb(root, f"y_acc{k}", [128, TOWN], F32) for k in range(KC)]
    comb_all = b.sb(root, "comb_all", [128, (TOWN // 128) * 8], F32)

    def fgp0(g):
        g.memset(ones[:], 1.0)
        g.memset(onesf[:], 1.0)
        g.memset(eps_t[:], LNEPS)
        g.memset(ident128[:], 0.0)
        g.drain()
        g.affine_select(out=ident128[:], in_=ident128[:], compare_op=ALU.not_equal,
                        fill=1.0, base=0, pattern=[[-1, 128]], channel_multiplier=1)
        for k in range(KC):
            g.memset(y_acc[k][:], 0.0)
    b.block.gpsimd(fgp0)

    with ExitStack() as stA:
        xsq = [b.sb(stA, f"xsq{k}", [128, TOWN], BF16) for k in range(KC)]
        s_x2 = [b.ctr("s_x0"), b.ctr("s_x1")]
        s_cast = b.ctr("s_cast")
        emit_load_cast_x(b, stA, rin, xbf, s_x2, s_cast, TOWN, also_square=xsq)
        barrier(b, bar, ("sync", "scalar", "gpsimd"))
        ln = {
            "ones": ones,
            "eps_t": eps_t,
            "mu_bc": b.sb(stA, "mu_bc", [128, 512], F32),
            "musq": b.sb(stA, "musq", [128, 512], F32),
            "var_bc": b.sb(stA, "var_bc", [128, 512], F32),
            "r_bc": b.sb(stA, "r_bc", [128, 512], F32),
            "tmp_sub": [b.sb(stA, f"tmp_sub{i}", [128, 512], F32) for i in range(2)],
            "pmu": b.ps(stA, "pmu", [128, 512]),
            "psq": b.ps(stA, "psq", [128, 512]),
        }
        emit_ln(b, stA, xbf, xsq, TOWN, ln)
        barrier(b, bar, ("tensor", "scalar", "vector", "gpsimd"))

    with ExitStack() as stG:
        wg_bf = b.sb(stG, "wg_bf", [128, KC * 8], BF16)
        wg_f = b.sb(stG, "wg_f", [128, KC * 8], F32)
        lg = b.sb(stG, "lg", [128, 8], F32)
        et = b.sb(stG, "et", [128, 8], F32)
        negm = b.sb(stG, "negm", [128, 1], F32)
        ssum = b.sb(stG, "ssum", [128, 1], F32)
        prob = b.sb(stG, "prob", [128, 8], F32)
        w1t = b.sb(stG, "w1t", [128, 1], F32)
        w2t = b.sb(stG, "w2t", [128, 1], F32)
        eqm = b.sb(stG, "eqm", [128, 8], F32)
        pmsk = b.sb(stG, "pmsk", [128, 8], F32)
        dsum = b.sb(stG, "dsum", [128, 1], F32)
        pg = b.ps(stG, "pg", [128, 512])
        s_gd = b.ctr("g_gd")
        s_pe = b.ctr("g_pe")
        s_ve = b.ctr("g_ve")
        s_se = b.ctr("g_se")

        def fsync(sync):
            ins = sync.dma_start(out=wg_f[:].rearrange("p (c e) -> p c e", e=8),
                                 in_=wg.rearrange("(c p) e -> p c e", p=128))
            s_gd.inc(ins, 16)
        b.block.sync(fsync)

        def fgp(g):
            g.wait_ge(s_gd.h, 16)
            g.tensor_copy(wg_bf[:], wg_f[:])
            g.drain()
            g.sem_inc(s_gd.h, 1)
        b.block.gpsimd(fgp)

        def ftensor(tensor):
            tensor.wait_ge(s_gd.h, 17)
            for mt in range(NMT):
                if mt >= 1:
                    tensor.wait_ge(s_ve.h, 12 * mt - 11)
                for k in range(KC):
                    ins = tensor.matmul(pg[:, 0:8], xbf[k][:, mt * 128:(mt + 1) * 128],
                                        wg_bf[:, k * 8:(k + 1) * 8],
                                        start=(k == 0), stop=(k == KC - 1))
                s_pe.inc(ins, 1)
        b.block.tensor(ftensor)

        def fscalar(scalar):
            for mt in range(NMT):
                scalar.wait_ge(s_ve.h, 12 * mt + 1)
                ins = scalar.activation(et[:], lg[:], AF.Exp, bias=negm[:])
                s_se.inc(ins, 1)
        b.block.scalar(fscalar)

        def fvector(vector):
            for mt in range(NMT):
                vector.wait_ge(s_pe.h, mt + 1)
                if mt >= 1:
                    vector.wait_ge(s_se.h, mt)
                vector.tensor_copy(lg[:], pg[:, 0:8])
                vector.drain()
                ins = vector.reduce_max(negm[:], lg[:], axis=AX.X, negate=True)
                s_ve.inc(ins, 1)
                vector.wait_ge(s_se.h, mt + 1)
                ins = vector.reduce_sum(ssum[:], et[:], axis=AX.X)
                s_ve.inc(ins, 1)
                vector.drain()
                ins = vector.reciprocal(ssum[:], ssum[:])
                s_ve.inc(ins, 1)
                vector.drain()
                ins = vector.tensor_scalar(prob[:], et[:], ssum[:], None, ALU.mult)
                s_ve.inc(ins, 1)
                vector.drain()
                ins = vector.reduce_max(w1t[:], prob[:], axis=AX.X)
                s_ve.inc(ins, 1)
                vector.drain()
                ins = vector.tensor_scalar(eqm[:], prob[:], w1t[:], None, ALU.is_ge)
                s_ve.inc(ins, 1)
                vector.drain()
                ins = vector.scalar_tensor_tensor(pmsk[:], eqm[:], -1e30, prob[:], ALU.mult, ALU.add)
                s_ve.inc(ins, 1)
                vector.drain()
                ins = vector.reduce_max(w2t[:], pmsk[:], axis=AX.X)
                s_ve.inc(ins, 1)
                vector.drain()
                ins = vector.tensor_tensor(dsum[:], w1t[:], w2t[:], ALU.add)
                s_ve.inc(ins, 1)
                vector.drain()
                ins = vector.tensor_scalar(dsum[:], dsum[:], 1e-8, None, ALU.add)
                s_ve.inc(ins, 1)
                vector.drain()
                ins = vector.reciprocal(dsum[:], dsum[:])
                s_ve.inc(ins, 1)
                vector.drain()
                vector.tensor_scalar(eqm[:], prob[:], w2t[:], None, ALU.is_ge)
                vector.drain()
                vector.tensor_tensor(eqm[:], prob[:], eqm[:], ALU.mult)
                vector.drain()
                ins = vector.tensor_scalar(comb_all[:, mt * 8:(mt + 1) * 8], eqm[:],
                                           dsum[:], None, ALU.mult)
                s_ve.inc(ins, 1)
        b.block.vector(fvector)
        barrier(b, bar, ("sync", "tensor", "scalar", "vector", "gpsimd"))

    with ExitStack() as stC:
        w1b = [b.sb(stC, f"w1b{k}", [128, 2048], BF16) for k in range(KC)]
        w2b = [b.sb(stC, f"w2b{k}", [128, 1024], BF16) for k in range(16)]
        wstg1 = [b.sb(stC, f"wstg1_{i}", [128, 1024], F32) for i in range(2)]
        wstg2 = [b.sb(stC, f"wstg2_{i}", [128, 1024], F32) for i in range(2)]
        hsq = b.sb(stC, "hsq", [128, 16 * 512], BF16)
        rl_sc = [b.sb(stC, f"rl_sc{i}", [128, 512], F32) for i in range(2)]
        crow = b.sb(stC, "crow", [1, 512], F32)
        cb_bc = b.sb(stC, "cb_bc", [128, 512], F32)
        ysc = b.sb(stC, "ysc", [128, 512], F32)
        ph = [b.ps(stC, f"ph{i}", [128, 512]) for i in range(2)]
        py = [b.ps(stC, f"py{i}", [128, 512]) for i in range(2)]
        pcb = b.ps(stC, "pcb", [128, 512])
        pcb2 = b.ps(stC, "pcb2", [128, 512])

        for e in range(NE):
            emit_expert(b, e, bar, w1, w2, w1b, w2b, wstg1, wstg2, hsq, rl_sc,
                        crow, cb_bc, ysc, ph, py, pcb, pcb2, xbf, y_acc, comb_all,
                        onesf, ident128, NTT)

    with ExitStack() as stD:
        rstg = [b.sb(stD, f"rstg{i}", [128, TOWN], F32) for i in range(2)]
        outc = [b.sb(stD, f"outc{i}", [128, TOWN], F32) for i in range(2)]
        s_rd = [b.ctr("d_rd0"), b.ctr("d_rd1")]
        s_ad = b.ctr("d_ad")
        s_st = [b.ctr("d_st0"), b.ctr("d_st1")]

        def fsync(sync):
            for k in range(KC):
                if k >= 2:
                    sync.wait_ge(s_ad.h, k - 1)
                ins = sync.dma_start(out=rstg[k % 2][:], in_=rin[k * 128:(k + 1) * 128, :])
                s_rd[k % 2].inc(ins, 16)
                if k >= 1:
                    sync.wait_ge(s_ad.h, k)
                    ins = sync.dma_start(out=rout[(k - 1) * 128:k * 128, :], in_=outc[(k - 1) % 2][:])
                    s_st[(k - 1) % 2].inc(ins, 16)
            sync.wait_ge(s_ad.h, KC)
            ins = sync.dma_start(out=rout[(KC - 1) * 128:KC * 128, :], in_=outc[(KC - 1) % 2][:])
            s_st[(KC - 1) % 2].inc(ins, 16)
            sync.wait_ge(s_st[0].h, s_st[0].n)
            sync.wait_ge(s_st[1].h, s_st[1].n)
        b.block.sync(fsync)

        def fvector(vector):
            for k in range(KC):
                vector.wait_ge(s_rd[k % 2].h, 16 * (k // 2 + 1))
                if k >= 2:
                    vector.wait_ge(s_st[k % 2].h, 16 * ((k - 2) // 2 + 1))
                ins = vector.tensor_tensor(outc[k % 2][:], rstg[k % 2][:], y_acc[k][:], ALU.add)
                s_ad.inc(ins, 1)
        b.block.vector(fvector)
        barrier(b, bar, ("sync", "vector"))


def emit_expert(b, e, bar, w1, w2, w1b, w2b, wstg1, wstg2, hsq, rl_sc,
                crow, cb_bc, ysc, ph, py, pcb, pcb2, xbf, y_acc, comb_all,
                onesf, ident128, NTT):
    nc = b.nc
    s_wd = b.ctr(f"e{e}_wd")
    s_wc = b.ctr(f"e{e}_wc")
    s_h = b.ctr(f"e{e}_h")
    s_rl = b.ctr(f"e{e}_rl")
    s_sq = b.ctr(f"e{e}_sq")
    s_y = b.ctr(f"e{e}_y")
    s_cb = b.ctr(f"e{e}_cb")
    s_cbs = b.ctr(f"e{e}_cbs")
    s_ya = b.ctr(f"e{e}_ya")

    def fgp(gp):
        i = 0
        for k in range(KC):
            for hh in range(2):
                if i >= 2:
                    gp.wait_ge(s_wc.h, i - 1)
                ins = gp.dma_start(out=wstg1[i % 2][:],
                                   in_=w1[e, k * 128:(k + 1) * 128, hh * 1024:(hh + 1) * 1024])
                s_wd.inc(ins, 16)
                gp.wait_ge(s_wd.h, s_wd.n)
                ins = gp.tensor_copy(w1b[k][:, hh * 1024:(hh + 1) * 1024], wstg1[i % 2][:])
                s_wc.inc(ins, 1)
                i += 1
        for k in range(16):
            gp.wait_ge(s_wc.h, i - 1)
            ins = gp.dma_start(out=wstg2[i % 2][:], in_=w2[e, k * 128:(k + 1) * 128, :])
            s_wd.inc(ins, 16)
            gp.wait_ge(s_wd.h, s_wd.n)
            ins = gp.tensor_copy(w2b[k][:], wstg2[i % 2][:])
            s_wc.inc(ins, 1)
            i += 1
    b.block.gpsimd(fgp)

    def ftensor(tensor):
        tensor.wait_ge(s_wc.h, 2 * KC)
        for tt in range(NTT):
            tsl = slice(tt * 512, (tt + 1) * 512)
            for mh in range(16):
                n = tt * 16 + mh
                if n >= 2:
                    tensor.wait_ge(s_rl.h, n - 1)
                for k in range(KC):
                    ins = tensor.matmul(ph[n % 2][:], w1b[k][:, mh * 128:(mh + 1) * 128],
                                        xbf[k][:, tsl], start=(k == 0), stop=(k == KC - 1))
                s_h.inc(ins, 1)
            for j in range(4):
                mt = tt * 4 + j
                tgt = 5 * tt + j
                if tgt > 0:
                    tensor.wait_ge(s_cbs.h, tgt)
                ins = tensor.transpose(pcb[0:1, (j % 2) * 128:(j % 2) * 128 + 128],
                                       comb_all[:, mt * 8 + e: mt * 8 + e + 1], ident128[:])
                s_cb.inc(ins, 1)
            tensor.wait_ge(s_cbs.h, tt * 5 + 4)
            ins = tensor.matmul(pcb2[:], onesf[0:1, :], crow[0:1, :], start=True, stop=True)
            s_cb.inc(ins, 1)
            tensor.wait_ge(s_wc.h, 2 * KC + 16)
            tensor.wait_ge(s_sq.h, (tt + 1) * 16)
            for md in range(KC):
                n = tt * KC + md
                if n >= 2:
                    tensor.wait_ge(s_ya.h, n - 1)
                for kh in range(16):
                    ins = tensor.matmul(py[n % 2][:], w2b[kh][:, md * 128:(md + 1) * 128],
                                        hsq[:, kh * 512:(kh + 1) * 512],
                                        start=(kh == 0), stop=(kh == 15))
                s_y.inc(ins, 1)
    b.block.tensor(ftensor)

    def fscalar(scalar):
        for tt in range(NTT):
            for mh in range(16):
                n = tt * 16 + mh
                scalar.wait_ge(s_h.h, n + 1)
                if n >= 2:
                    scalar.wait_ge(s_sq.h, n - 1)
                ins = scalar.activation(rl_sc[n % 2][:], ph[n % 2][:], AF.Relu)
                s_rl.inc(ins, 1)
            for j in range(4):
                scalar.wait_ge(s_cb.h, tt * 5 + j + 1)
                ins = scalar.copy(crow[0:1, j * 128:(j + 1) * 128],
                                  pcb[0:1, (j % 2) * 128:(j % 2) * 128 + 128])
                s_cbs.inc(ins, 1)
            scalar.wait_ge(s_cb.h, tt * 5 + 5)
            if tt >= 1:
                scalar.wait_ge(s_ya.h, tt * KC)
            ins = scalar.copy(cb_bc[:], pcb2[:])
            s_cbs.inc(ins, 1)
    b.block.scalar(fscalar)

    def fvector(vector):
        for tt in range(NTT):
            for mh in range(16):
                n = tt * 16 + mh
                vector.wait_ge(s_rl.h, n + 1)
                ins = vector.tensor_tensor(hsq[:, mh * 512:(mh + 1) * 512],
                                           rl_sc[n % 2][:], rl_sc[n % 2][:], ALU.mult)
                s_sq.inc(ins, 1)
            for md in range(KC):
                n = tt * KC + md
                vector.wait_ge(s_y.h, n + 1)
                vector.wait_ge(s_cbs.h, 5 * (tt + 1))
                vector.drain()
                ins = vector.tensor_tensor(ysc[:], py[n % 2][:], cb_bc[:], ALU.mult)
                vector.drain()
                ins = vector.tensor_tensor(y_acc[md][:, tt * 512:(tt + 1) * 512],
                                           y_acc[md][:, tt * 512:(tt + 1) * 512],
                                           ysc[:], ALU.add)
                s_ya.inc(ins, 1)
        vector.drain()
    b.block.vector(fvector)
    barrier(b, bar, ("tensor", "scalar", "vector", "gpsimd"))



TEXT = 2560
HB = 32


def build_bitffn_layer():
    nc = bass.Bass("TRN2", target_bir_lowering=False)
    rin = nc.dram_tensor("rin", [D, TEXT], F32, kind="ExternalInput").ap()
    w1T = nc.dram_tensor("w1T", [D, 4096], F32, kind="ExternalInput").ap()
    cw = nc.dram_tensor("cw", [128, HB * 3], F32, kind="ExternalInput").ap()
    cb = nc.dram_tensor("cb", [128, HB], F32, kind="ExternalInput").ap()
    w2T = nc.dram_tensor("w2T", [4096, D], F32, kind="ExternalInput").ap()
    rout = nc.dram_tensor("rout", [D, TOWN], F32, kind="ExternalOutput").ap()
    h2sc = nc.dram_tensor("h2sc", [128, 4 * HB * 512], BF16, kind="Internal").ap()
    with ExitStack() as ctx:
        blk = ctx.enter_context(nc.Block())
        b = Bld(nc, ctx, blk)
        emit_bitffn_layer(b, ctx, rin, w1T, cw, cb, w2T, rout, h2sc)
    return nc


def emit_absmax_rows(b, name, bar, chunks, nch, ncols, mxrow, srow, ident128, st,
                     ctrs=None):
    if ctrs is None:
        ctrs = {nm: b.ctr(name + "_" + nm) for nm in ("t1", "t2", "t3", "ab", "abc")}
    s_t1, s_t2, s_t3, s_ab, s_abc = (ctrs[n] for n in ("t1", "t2", "t3", "ab", "abc"))
    b1, b2, b3, bA, bAC = s_t1.n, s_t2.n, s_t3.n, s_ab.n, s_abc.n
    mx = b.sb(st, name + "_mx", [128, 512], F32)
    ab = [b.sb(st, name + f"_ab{i}", [128, 512], BF16) for i in range(2)]
    tcol = b.sb(st, name + "_tcol", [128, 2], F32)
    ptile = b.ps(st, name + "_pt", [128, 512])
    NCT = ncols // 512


    def fvector(vector):
        for ct in range(NCT):
            if ct >= 1:
                vector.wait_ge(s_t2.h, b2 + 12 * ct)
            for k in range(nch):
                n = ct * nch + k
                vector.wait_ge(s_ab.h, bA + n + 1)
                vector.drain()
                if k == 0:
                    ins = vector.tensor_copy(mx[:], ab[0][:])
                else:
                    ins = vector.tensor_tensor(mx[:], mx[:], ab[k % 2][:], ALU.max)
                s_abc.inc(ins, 1)
            vector.drain()
            ins = vector.tensor_scalar(mx[:], mx[:], 1e-5, None, ALU.max)
            s_t1.inc(ins, 1)
            for j in range(4):
                qq = ct * 4 + j
                vector.wait_ge(s_t2.h, b2 + 3 * qq + 1)
                vector.drain()
                vector.reduce_max(tcol[:, 0:1], ptile[0:128, 0:128], axis=AX.X)
                vector.drain()
                vector.reciprocal(tcol[:, 1:2], tcol[:, 0:1])
                vector.drain()
                ins = vector.tensor_scalar(tcol[:, 1:2], tcol[:, 1:2], 127.0, None, ALU.mult)
                s_t1.inc(ins, 1)
    b.block.vector(fvector)

    def ftensor(tensor):
        for ct in range(NCT):
            for j in range(4):
                qq = ct * 4 + j
                tensor.wait_ge(s_t1.h, b1 + 5 * ct + 1)
                if qq >= 1:
                    tensor.wait_ge(s_t3.h, b3 + qq)
                ins = tensor.transpose(ptile[0:128, 0:128], mx[:, j * 128:(j + 1) * 128],
                                       ident128[:])
                s_t2.inc(ins, 1)
                tensor.wait_ge(s_t1.h, b1 + 5 * ct + 1 + j + 1)
                ins = tensor.transpose(ptile[0:1, 128:256], tcol[:, 0:1], ident128[:])
                s_t2.inc(ins, 1)
                ins = tensor.transpose(ptile[0:1, 256:384], tcol[:, 1:2], ident128[:])
                s_t2.inc(ins, 1)
    b.block.tensor(ftensor)

    def fscalar(scalar):
        for ct in range(NCT):
            sl = slice(ct * 512, (ct + 1) * 512)
            for k in range(nch):
                n = ct * nch + k
                if n >= 2:
                    scalar.wait_ge(s_abc.h, bAC + n - 1)
                ins = scalar.activation(ab[k % 2][:], chunks[k][:, sl], AF.Abs)
                s_ab.inc(ins, 1)
            for j in range(4):
                qq = ct * 4 + j
                scalar.wait_ge(s_t2.h, b2 + 3 * qq + 3)
                scalar.copy(mxrow[0:1, qq * 128:(qq + 1) * 128], ptile[0:1, 128:256])
                ins = scalar.copy(srow[0:1, qq * 128:(qq + 1) * 128], ptile[0:1, 256:384])
                s_t3.inc(ins, 1)
    b.block.scalar(fscalar)
    barrier(b, bar, ("tensor", "scalar", "vector"))


def emit_actquant(b, name, bar, chunks, nch, ncols, srow, onesb, qsc, ctrs=None):
    if ctrs is None:
        ctrs = {nm: b.ctr(name + "_" + nm) for nm in ("bc", "v", "g")}
    s_bc, s_v, s_g = ctrs["bc"], ctrs["v"], ctrs["g"]
    qbc, qv, qg = s_bc.n, s_v.n, s_g.n
    sbc, t1, q8, r0, dd, aa, bb2, pb = (qsc[k] for k in
                                        ("sbc", "t1", "q8", "r0", "dd", "aa", "bb2", "pb"))
    NCT = ncols // 512

    def ftensor(tensor):
        for ct in range(NCT):
            if ct >= 1:
                tensor.wait_ge(s_v.h, qv + 7 * nch * ct)
            ins = tensor.matmul(pb[:], onesb[0:1, :], srow[0:1, ct * 512:(ct + 1) * 512],
                                start=True, stop=True)
            s_bc.inc(ins, 1)
    b.block.tensor(ftensor)

    def fscalar(scalar):
        for ct in range(NCT):
            scalar.wait_ge(s_bc.h, qbc + 2 * ct + 1)
            scalar.copy(sbc[:], pb[:])
            scalar.drain()
            scalar.sem_inc(s_bc.h, 1)
            s_bc.n += 1
    b.block.scalar(fscalar)

    def fvector(vector):
        for ct in range(NCT):
            sl = slice(ct * 512, (ct + 1) * 512)
            vector.wait_ge(s_bc.h, qbc + 2 * (ct + 1))
            for k in range(nch):
                n = ct * nch + k
                if n >= 1:
                    vector.wait_ge(s_g.h, qg + n)
                vector.drain()
                vector.tensor_tensor(t1[:], chunks[k][:, sl], sbc[:], ALU.mult)
                vector.drain()
                vector.tensor_copy(q8[:], t1[:])
                vector.drain()
                vector.tensor_copy(r0[:], q8[:])
                vector.drain()
                vector.tensor_tensor(dd[:], t1[:], r0[:], ALU.subtract)
                vector.drain()
                vector.tensor_scalar(aa[:], dd[:], 0.5, None, ALU.is_ge)
                vector.drain()
                vector.tensor_scalar(bb2[:], dd[:], -0.5, None, ALU.is_le)
                vector.drain()
                ins = vector.tensor_tensor(aa[:], r0[:], aa[:], ALU.add)
                s_v.inc(ins, 7)
    b.block.vector(fvector)

    def fgp(g):
        for ct in range(NCT):
            sl = slice(ct * 512, (ct + 1) * 512)
            for k in range(nch):
                n = ct * nch + k
                g.wait_ge(s_v.h, qv + 7 * (n + 1))
                ins = g.tensor_tensor(chunks[k][:, sl], aa[:], bb2[:], ALU.subtract)
                g.drain()
                s_g.inc(ins, 1)
    b.block.gpsimd(fgp)
    barrier(b, bar, ("tensor", "scalar", "vector", "gpsimd"))


def emit_wquant(b, name, bar, wT, nchunk, wcols, wq, wstg, wmean_sb, ident128,
                onesf, onescol, wsc):
    s_d2 = [b.ctr(name + "_d0"), b.ctr(name + "_d1")]
    s_v = b.ctr(name + "_v")
    s_p = b.ctr(name + "_p")
    s_s = b.ctr(name + "_s")
    s_sg = b.ctr(name + "_sg")
    s_q = b.ctr(name + "_q")
    absac, swcol, t1, sg, pbank = (wsc[k] for k in ("absac", "swcol", "t1", "sg", "pbank"))
    nel = float(nchunk * 128 * wcols)

    def fsyncA(sync):
        for k in range(nchunk):
            if k >= 2:
                sync.wait_ge(s_v.h, k - 1)
            ins = sync.dma_start(out=wstg[k % 2][:], in_=wT[k * 128:(k + 1) * 128, :])
            s_d2[k % 2].inc(ins, 16)
    b.block.sync(fsyncA)

    def fvectorA(vector):
        for k in range(nchunk):
            vector.wait_ge(s_d2[k % 2].h, 16 * (k // 2 + 1))
            vector.drain()
            vector.reduce_sum(t1[:, 0:1], wstg[k % 2][:], axis=AX.X,
                              apply_absolute_value=True)
            vector.drain()
            if k == 0:
                ins = vector.tensor_copy(absac[:], t1[:, 0:1])
            else:
                ins = vector.tensor_tensor(absac[:], absac[:], t1[:, 0:1], ALU.add)
            vector.drain()
            s_v.inc(ins, 1)
        vector.wait_ge(s_s.h, 1)
        vector.drain()
        ins = vector.tensor_scalar(wmean_sb[0:1, 0:1], wmean_sb[0:1, 0:1], 1e-5, None, ALU.max)
        s_v.inc(ins, 1)
        vector.drain()
        ins = vector.reciprocal(swcol[0:1, 0:1], wmean_sb[0:1, 0:1])
        s_v.inc(ins, 1)
    b.block.vector(fvectorA)

    def ftensorA(tensor):
        tensor.wait_ge(s_v.h, nchunk)
        ins = tensor.matmul(pbank[0:1, 0:1], absac[:], onescol[:], start=True, stop=True)
        s_p.inc(ins, 1)
        tensor.wait_ge(s_v.h, nchunk + 2)
        ins = tensor.matmul(pbank[0:128, 128:129], onesf[0:1, :], swcol[0:1, 0:1],
                            start=True, stop=True)
        s_p.inc(ins, 1)
    b.block.tensor(ftensorA)

    def fscalarA(scalar):
        scalar.wait_ge(s_p.h, 1)
        ins = scalar.activation(wmean_sb[0:1, 0:1], pbank[0:1, 0:1], AF.Copy, scale=1.0 / nel)
        s_s.inc(ins, 1)
        scalar.wait_ge(s_p.h, 2)
        ins = scalar.copy(swcol[:], pbank[0:128, 128:129])
        s_s.inc(ins, 1)
    b.block.scalar(fscalarA)
    barrier(b, bar, ("sync", "tensor", "vector", "scalar"))

    cbase = [s_d2[0].n, s_d2[1].n]

    def fsyncB(sync):
        for k in range(nchunk):
            if k >= 2:
                sync.wait_ge(s_q.h, k - 1)
            ins = sync.dma_start(out=wstg[k % 2][:], in_=wT[k * 128:(k + 1) * 128, :])
            s_d2[k % 2].inc(ins, 16)
    b.block.sync(fsyncB)

    vbase = s_v.n

    ma, mb = wsc["ma"], wsc["mb"]

    def fvectorB(vector):
        for k in range(nchunk):
            vector.wait_ge(s_d2[k % 2].h, cbase[k % 2] + 16 * (k // 2 + 1))
            if k >= 1:
                vector.wait_ge(s_q.h, k)
            vector.drain()
            ins = vector.tensor_scalar(t1[:], wstg[k % 2][:], swcol[:, 0:1], None, ALU.mult)
            s_v.inc(ins, 1)
            vector.drain()
            vector.tensor_scalar(ma[:], t1[:], 0.5, None, ALU.is_ge)
            vector.drain()
            vector.tensor_scalar(mb[:], t1[:], -0.5, None, ALU.is_le)
            vector.drain()
            ins = vector.tensor_tensor(ma[:], ma[:], mb[:], ALU.add)
            s_v.inc(ins, 1)
    b.block.vector(fvectorB)

    def fscalarB(scalar):
        for k in range(nchunk):
            scalar.wait_ge(s_v.h, vbase + 2 * k + 1)
            if k >= 1:
                scalar.wait_ge(s_q.h, k)
            scalar.drain()
            ins = scalar.activation(sg[:], t1[:], AF.Sign)
            s_sg.inc(ins, 1)
    b.block.scalar(fscalarB)

    def fgpB(g):
        for k in range(nchunk):
            g.wait_ge(s_sg.h, k + 1)
            g.wait_ge(s_v.h, vbase + 2 * k + 2)
            ins = g.tensor_tensor(wq[k][:], sg[:], ma[:], ALU.mult)
            g.drain()
            s_q.inc(ins, 1)
    b.block.gpsimd(fgpB)
    barrier(b, bar, ("sync", "tensor", "vector", "scalar", "gpsimd"))


def emit_bitffn_layer(b, root, rin, w1T, cw, cb, w2T, rout, h2sc):
    nc = b.nc
    bar = b.ctr("bar")
    ones = b.sb(root, "ones", [128, 128], BF16)
    onesf = b.sb(root, "onesf", [1, 128], F32)
    onescol = b.sb(root, "onescol", [128, 1], F32)
    eps_t = b.sb(root, "eps_t", [128, 1], F32)
    ident128 = b.sb(root, "ident128", [128, 128], F32)
    xbf = [b.sb(root, f"xbf{k}", [128, TEXT], BF16) for k in range(KC)]
    mx1row = b.sb(root, "mx1row", [1, TEXT], F32)
    mx2row = b.sb(root, "mx2row", [1, TOWN], F32)
    wm1 = b.sb(root, "wm1", [1, 1], F32)
    wm2 = b.sb(root, "wm2", [1, 1], F32)
    cwt = b.sb(root, "cwt", [128, HB * 3], F32)
    cbt = b.sb(root, "cbt", [128, HB], F32)
    s_misc = b.ctr("s_misc")

    def fsync0(sync):
        ins = sync.dma_start(out=cwt[:], in_=cw)
        s_misc.inc(ins, 16)
        ins = sync.dma_start(out=cbt[:], in_=cb)
        s_misc.inc(ins, 16)
    b.block.sync(fsync0)

    def fgp0(g):
        g.memset(ones[:], 1.0)
        g.memset(onesf[:], 1.0)
        g.memset(onescol[:], 1.0)
        g.memset(eps_t[:], LNEPS)
        g.memset(ident128[:], 0.0)
        g.drain()
        g.affine_select(out=ident128[:], in_=ident128[:], compare_op=ALU.not_equal,
                        fill=1.0, base=0, pattern=[[-1, 128]], channel_multiplier=1)
        g.wait_ge(s_misc.h, 32)
    b.block.gpsimd(fgp0)

    with ExitStack() as stA:
        xsq = [b.sb(stA, f"xsq{k}", [128, TEXT], BF16) for k in range(KC)]
        s_x2 = [b.ctr("s_x0"), b.ctr("s_x1")]
        s_cast = b.ctr("s_cast")
        emit_load_cast_x(b, stA, rin, xbf, s_x2, s_cast, TEXT, also_square=xsq)
        barrier(b, bar, ("sync", "scalar", "gpsimd"))
        ln = {
            "ones": ones,
            "eps_t": eps_t,
            "mu_bc": b.sb(stA, "mu_bc", [128, 512], F32),
            "musq": b.sb(stA, "musq", [128, 512], F32),
            "var_bc": b.sb(stA, "var_bc", [128, 512], F32),
            "r_bc": b.sb(stA, "r_bc", [128, 512], F32),
            "tmp_sub": [b.sb(stA, f"tmp_sub{i}", [128, 512], F32) for i in range(2)],
            "pmu": b.ps(stA, "pmu", [128, 512]),
            "psq": b.ps(stA, "psq", [128, 512]),
        }
        emit_ln(b, stA, xbf, xsq, TEXT, ln)
        barrier(b, bar, ("tensor", "scalar", "vector", "gpsimd"))

    with ExitStack() as stQ:
        s1row = b.sb(stQ, "s1row", [1, TEXT], BF16)
        emit_absmax_rows(b, "q1", bar, xbf, KC, TEXT, mx1row, s1row, ident128, stQ)
        qsc = {
            "sbc": b.sb(stQ, "q_sbc", [128, 512], BF16),
            "t1": b.sb(stQ, "q_t1", [128, 512], F32),
            "q8": b.sb(stQ, "q_q8", [128, 512], I8),
            "r0": b.sb(stQ, "q_r0", [128, 512], BF16),
            "dd": b.sb(stQ, "q_dd", [128, 512], F32),
            "aa": b.sb(stQ, "q_aa", [128, 512], BF16),
            "bb2": b.sb(stQ, "q_bb2", [128, 512], BF16),
            "pb": b.ps(stQ, "q_pb", [128, 512]),
        }
        emit_actquant(b, "aq1", bar, xbf, KC, TEXT, s1row, ones, qsc)

    with ExitStack() as stH:
        wq1 = [b.sb(stH, f"wq1_{k}", [128, 4096], BF16) for k in range(KC)]
        with ExitStack() as stWq:
            wstg = [b.sb(stWq, f"wstg{i}", [128, 4096], F32) for i in range(2)]
            wsc = {
                "absac": b.sb(stWq, "w_absac", [128, 1], F32),
                "swcol": b.sb(stWq, "w_swcol", [128, 1], F32),
                "t1": b.sb(stWq, "w_t1", [128, 4096], F32),
                "sg": b.sb(stWq, "w_sg", [128, 4096], BF16),
                "ma": b.sb(stWq, "w_ma", [128, 4096], BF16),
                "mb": b.sb(stWq, "w_mb", [128, 4096], BF16),
                "pbank": b.ps(stWq, "w_pb", [128, 512]),
            }
            emit_wquant(b, "w1q", bar, w1T, KC, 4096, wq1, wstg, wm1, ident128,
                        onesf, onescol, wsc)

        def fos1(g):
            g.tensor_scalar(mx1row[0:1, :], mx1row[0:1, :], wm1[0:1, 0:1], 1.0 / 127.0,
                            ALU.mult, ALU.mult)
        b.block.gpsimd(fos1)
        barrier(b, bar, ("gpsimd",))
        os1row = mx1row

        h_buf = b.sb(stH, "h_buf", [128, HB * 512], BF16)
        silu_buf = b.sb(stH, "silu_buf", [128, HB * 512], BF16)
        edgeb = b.sb(stH, "edgeb", [128, HB * 5], BF16)
        lco = b.sb(stH, "lco", [128, HB], BF16)
        xqe = [b.sb(stH, f"xqe{k}", [128, 5], BF16) for k in range(KC)]
        ose = b.sb(stH, "ose", [1, 5], F32)
        oseb = b.sb(stH, "oseb", [128, 5], F32)
        phh = [b.ps(stH, f"phh{i}", [128, 512]) for i in range(2)]
        pob = b.ps(stH, "pob", [128, 512])

        with ExitStack() as stE0:
            obc0 = b.sb(stE0, "obc_e", [128, 512], F32)
            emit_edge_cols(b, bar, wq1, xbf, xqe, ose, oseb, edgeb, onesf, os1row, phh, pob)

        hctrs = {nm: b.ctr("ht_" + nm) for nm in ("pe", "ob", "ev")}
        cctrs = {nm: b.ctr("cv_" + nm) for nm in ("c1", "c2", "c3", "sl")}
        q2ctrs = {nm: b.ctr("q2_" + nm) for nm in ("t1", "t2", "t3", "ab", "abc")}
        aq2ctrs = {nm: b.ctr("aq2_" + nm) for nm in ("bc", "v", "g")}
        s_h2 = b.ctr("h2st")
        for tt in range(4):
            with ExitStack() as stCV:
                obc = b.sb(stCV, f"obc_t{tt}", [128, 512], F32)
                ca = [b.sb(stCV, f"ca{tt}_{i}", [128, 512], F32) for i in range(2)]
                emit_h_tile(b, bar, tt, wq1, xbf, h_buf, obc, pob, phh, os1row, onesf, hctrs)
                emit_conv_silu(b, bar, tt, h_buf, edgeb, lco, ca, silu_buf, cwt, cbt, cctrs)
            with ExitStack() as stq2:
                s2tt = b.sb(stq2, f"s2tt{tt}", [1, 512], BF16)
                sl_chunks = [silu_buf[:, ch * 512:(ch + 1) * 512] for ch in range(HB)]
                emit_absmax_rows(b, f"q2_{tt}", bar, sl_chunks, HB, 512,
                                 mx2row[0:1, tt * 512:(tt + 1) * 512],
                                 s2tt[0:1, :], ident128, stq2, ctrs=q2ctrs)
                qsc2 = {
                    "sbc": b.sb(stq2, f"q2_sbc_{tt}", [128, 512], BF16),
                    "t1": b.sb(stq2, f"q2_t1_{tt}", [128, 512], F32),
                    "q8": b.sb(stq2, f"q2_q8_{tt}", [128, 512], I8),
                    "r0": b.sb(stq2, f"q2_r0_{tt}", [128, 512], BF16),
                    "dd": b.sb(stq2, f"q2_dd_{tt}", [128, 512], F32),
                    "aa": b.sb(stq2, f"q2_aa_{tt}", [128, 512], BF16),
                    "bb2": b.sb(stq2, f"q2_bb2_{tt}", [128, 512], BF16),
                    "pb": b.ps(stq2, f"q2_pb_{tt}", [128, 512]),
                }
                emit_actquant(b, f"aq2_{tt}", bar, sl_chunks, HB, 512,
                              s2tt[0:1, :], ones, qsc2, ctrs=aq2ctrs)

            def fst(sync, tt=tt, s_h2=s_h2):
                tgt = s_h2.n + 16
                ins = sync.dma_start(out=h2sc[:, tt * HB * 512:(tt + 1) * HB * 512],
                                     in_=silu_buf[:])
                s_h2.inc(ins, 16)
                sync.wait_ge(s_h2.h, tgt)
            b.block.sync(fst)
            barrier(b, bar, ("sync",))

    with ExitStack() as stW2:
        wq2 = [b.sb(stW2, f"wq2_{k}", [128, 1024], BF16) for k in range(HB)]
        with ExitStack() as stWq2:
            wstg2 = [b.sb(stWq2, f"wstg2_{i}", [128, 1024], F32) for i in range(2)]
            wsc2 = {
                "absac": b.sb(stWq2, "w2_absac", [128, 1], F32),
                "swcol": b.sb(stWq2, "w2_swcol", [128, 1], F32),
                "t1": b.sb(stWq2, "w2_t1", [128, 1024], F32),
                "sg": b.sb(stWq2, "w2_sg", [128, 1024], BF16),
                "ma": b.sb(stWq2, "w2_ma", [128, 1024], BF16),
                "mb": b.sb(stWq2, "w2_mb", [128, 1024], BF16),
                "pbank": b.ps(stWq2, "w2_pb", [128, 512]),
            }
            emit_wquant(b, "w2q", bar, w2T, HB, 1024, wq2, wstg2, wm2, ident128,
                        onesf, onescol, wsc2)

        def fos2(g):
            g.tensor_scalar(mx2row[0:1, :], mx2row[0:1, :], wm2[0:1, 0:1], 1.0 / 127.0,
                            ALU.mult, ALU.mult)
        b.block.gpsimd(fos2)
        barrier(b, bar, ("gpsimd",))
        os2row = mx2row

        h2t = [b.sb(stW2, f"h2t{i}", [128, HB * 512], BF16) for i in range(2)]
        rstg = [b.sb(stW2, f"rstg{i}", [128, 512], F32) for i in range(2)]
        obc2 = b.sb(stW2, "obc2", [128, 512], F32)
        ysc2 = [b.sb(stW2, f"ysc2_{i}", [128, 512], F32) for i in range(2)]
        outsl = [b.sb(stW2, f"outsl{i}", [128, 512], F32) for i in range(2)]
        pyy = [b.ps(stW2, f"pyy{i}", [128, 512]) for i in range(2)]
        pob2 = b.ps(stW2, "pob2", [128, 512])
        emit_bitlin2(b, bar, wq2, h2t, rstg, obc2, ysc2, outsl, pyy, pob2,
                     os2row, onesf, rin, rout, h2sc)


def emit_edge_cols(b, bar, wq1, xq, xqe, ose, oseb, edgeb, onesf, os1row, phh, pob):
    cols = [512, 1024, 1536, 2048, 2049]
    s_g = b.ctr("ec_g")
    s_pe = b.ctr("ec_pe")
    s_sc = b.ctr("ec_sc")
    s_ev = b.ctr("ec_ev")

    def fgp(g):
        for k in range(KC):
            for j, c in enumerate(cols):
                g.tensor_copy(xqe[k][:, j:j + 1], xq[k][:, c:c + 1])
        g.drain()
        for j, c in enumerate(cols):
            g.tensor_copy(ose[0:1, j:j + 1], os1row[0:1, c:c + 1])
        g.drain()
        g.sem_inc(s_g.h, 1)
    b.block.gpsimd(fgp)

    def ftensor(tensor):
        tensor.wait_ge(s_g.h, 1)
        ins = tensor.matmul(pob[:, 0:5], onesf[0:1, :], ose[0:1, :], start=True, stop=True)
        s_pe.inc(ins, 1)
        for ch in range(HB):
            if ch >= 2:
                tensor.wait_ge(s_ev.h, ch - 1)
            for k in range(KC):
                ins = tensor.matmul(phh[ch % 2][:, 0:5], wq1[k][:, ch * 128:(ch + 1) * 128],
                                    xqe[k][:], start=(k == 0), stop=(k == KC - 1))
            s_pe.inc(ins, 1)
    b.block.tensor(ftensor)

    def fscalar(scalar):
        scalar.wait_ge(s_pe.h, 1)
        ins = scalar.copy(oseb[:], pob[:, 0:5])
        scalar.drain()
        scalar.sem_inc(s_sc.h, 1)
        s_sc.n += 1
    b.block.scalar(fscalar)

    def fvector(vector):
        vector.wait_ge(s_sc.h, 1)
        for ch in range(HB):
            vector.wait_ge(s_pe.h, ch + 2)
            ins = vector.tensor_tensor(edgeb[:, ch * 5:(ch + 1) * 5], phh[ch % 2][:, 0:5],
                                       oseb[:], ALU.mult)
            s_ev.inc(ins, 1)
    b.block.vector(fvector)
    barrier(b, bar, ("tensor", "scalar", "vector", "gpsimd"))


def emit_h_tile(b, bar, tt, wq1, xq, h_buf, obc, pob, phh, os1row, onesf, ctrs):
    s_pe, s_ob, s_ev = ctrs["pe"], ctrs["ob"], ctrs["ev"]
    hpe, hob, hev = s_pe.n, s_ob.n, s_ev.n
    tsl = slice(tt * 512, (tt + 1) * 512)

    def ftensor(tensor):
        ins = tensor.matmul(pob[:], onesf[0:1, :], os1row[0:1, tsl], start=True, stop=True)
        s_ob.inc(ins, 1)
        for ch in range(HB):
            if ch >= 2:
                tensor.wait_ge(s_ev.h, hev + ch - 1)
            for k in range(KC):
                ins = tensor.matmul(phh[ch % 2][:], wq1[k][:, ch * 128:(ch + 1) * 128],
                                    xq[k][:, tsl], start=(k == 0), stop=(k == KC - 1))
            s_pe.inc(ins, 1)
    b.block.tensor(ftensor)

    def fscalar(scalar):
        scalar.wait_ge(s_ob.h, hob + 1)
        scalar.copy(obc[:], pob[:])
        scalar.drain()
        scalar.sem_inc(s_ob.h, 1)
        s_ob.n += 1
    b.block.scalar(fscalar)

    def fvector(vector):
        vector.wait_ge(s_ob.h, hob + 2)
        for ch in range(HB):
            vector.wait_ge(s_pe.h, hpe + ch + 1)
            ins = vector.tensor_tensor(h_buf[:, ch * 512:(ch + 1) * 512],
                                       phh[ch % 2][:], obc[:], ALU.mult)
            s_ev.inc(ins, 1)
    b.block.vector(fvector)
    barrier(b, bar, ("tensor", "scalar", "vector"))


def emit_conv_silu(b, bar, tt, h_buf, edgeb, lco, ca, silu_buf, cwt, cbt, ctrs):
    s_c1, s_c2, s_c3, s_sl = ctrs["c1"], ctrs["c2"], ctrs["c3"], ctrs["sl"]
    bc1, bc2, bc3, bsl = s_c1.n, s_c2.n, s_c3.n, s_sl.n
    ridx = tt if tt < 3 else 4
    lidx = 3

    def fvector(vector):
        for ch in range(HB):
            c = slice(ch * 512, (ch + 1) * 512)
            if ch >= 2:
                vector.wait_ge(s_sl.h, bsl + ch - 1)
            vector.drain()
            vector.tensor_scalar(ca[ch % 2][:], h_buf[:, c],
                                 cwt[:, 3 * ch + 1:3 * ch + 2], None, ALU.mult)
            vector.drain()
            vector.scalar_tensor_tensor(ca[ch % 2][:, 1:512], h_buf[:, c][:, 0:511],
                                        cwt[:, 3 * ch:3 * ch + 1],
                                        ca[ch % 2][:, 1:512], ALU.mult, ALU.add)
            vector.drain()
            lsrc = (lco[:, ch:ch + 1] if tt > 0 else edgeb[:, ch * 5 + lidx:ch * 5 + lidx + 1])
            vector.scalar_tensor_tensor(ca[ch % 2][:, 0:1], lsrc,
                                        cwt[:, 3 * ch:3 * ch + 1],
                                        ca[ch % 2][:, 0:1], ALU.mult, ALU.add)
            vector.drain()
            vector.scalar_tensor_tensor(ca[ch % 2][:, 0:511], h_buf[:, c][:, 1:512],
                                        cwt[:, 3 * ch + 2:3 * ch + 3],
                                        ca[ch % 2][:, 0:511], ALU.mult, ALU.add)
            vector.drain()
            rsrc = edgeb[:, ch * 5 + ridx:ch * 5 + ridx + 1]
            ins = vector.scalar_tensor_tensor(ca[ch % 2][:, 511:512], rsrc,
                                              cwt[:, 3 * ch + 2:3 * ch + 3],
                                              ca[ch % 2][:, 511:512], ALU.mult, ALU.add)
            s_c3.inc(ins, 1)
    b.block.vector(fvector)

    def fscalar(scalar):
        for ch in range(HB):
            scalar.wait_ge(s_c3.h, bc3 + ch + 1)
            ins = scalar.activation(silu_buf[:, ch * 512:(ch + 1) * 512], ca[ch % 2][:],
                                    AF.Silu, bias=cbt[:, ch:ch + 1])
            s_sl.inc(ins, 1)
    b.block.scalar(fscalar)
    barrier(b, bar, ("vector", "gpsimd", "scalar"))

    def fsave(v):
        src = h_buf[:].rearrange("p (c t) -> p c t", t=512)
        v.tensor_copy(lco[:], src[:, :, 511])
    b.block.vector(fsave)
    barrier(b, bar, ("vector",))


def emit_bitlin2(b, bar, wq2, h2t, rstg, obc2, ysc2, outsl, pyy, pob2,
                 os2row, onesf, rin, rout, h2sc):
    s_hd = [b.ctr("y_hd0"), b.ctr("y_hd1")]
    s_rd = [b.ctr("y_rd0"), b.ctr("y_rd1")]
    s_ob = b.ctr("y_ob")
    s_pe = b.ctr("y_pe")
    s_ev = b.ctr("y_ev")
    s_st = [b.ctr("y_st0"), b.ctr("y_st1")]
    NYT = 4

    def fsync(sync):
        for tt in range(NYT):
            if tt >= 2:
                sync.wait_ge(s_pe.h, (tt - 1) * KC)
            ins = sync.dma_start(out=h2t[tt % 2][:],
                                 in_=h2sc[:, tt * HB * 512:(tt + 1) * HB * 512])
            s_hd[tt % 2].inc(ins, 16)
            for md in range(KC):
                n = tt * KC + md
                if n >= 2:
                    sync.wait_ge(s_ev.h, 2 * (n - 1))
                ins = sync.dma_start(out=rstg[n % 2][:],
                                     in_=rin[md * 128:(md + 1) * 128, tt * 512:(tt + 1) * 512])
                s_rd[n % 2].inc(ins, 16)
                if n >= 1:
                    sync.wait_ge(s_ev.h, 2 * n)
                    ins = sync.dma_start(
                        out=rout[((n - 1) % KC) * 128:((n - 1) % KC + 1) * 128,
                                 ((n - 1) // KC) * 512:((n - 1) // KC + 1) * 512],
                        in_=outsl[(n - 1) % 2][:])
                    s_st[(n - 1) % 2].inc(ins, 16)
        n = NYT * KC
        sync.wait_ge(s_ev.h, 2 * n)
        ins = sync.dma_start(out=rout[(KC - 1) * 128:KC * 128, (NYT - 1) * 512:NYT * 512],
                             in_=outsl[(n - 1) % 2][:])
        s_st[(n - 1) % 2].inc(ins, 16)
        sync.wait_ge(s_st[0].h, s_st[0].n)
        sync.wait_ge(s_st[1].h, s_st[1].n)
    b.block.sync(fsync)

    def ftensor(tensor):
        for tt in range(NYT):
            tensor.wait_ge(s_hd[tt % 2].h, 16 * (tt // 2 + 1))
            if tt >= 1:
                tensor.wait_ge(s_ob.h, 2 * tt)
            ins = tensor.matmul(pob2[:], onesf[0:1, :], os2row[0:1, tt * 512:(tt + 1) * 512],
                                start=True, stop=True)
            s_ob.inc(ins, 1)
            for md in range(KC):
                n = tt * KC + md
                if n >= 2:
                    tensor.wait_ge(s_ev.h, 2 * (n - 1))
                for kh in range(HB):
                    ins = tensor.matmul(pyy[n % 2][:], wq2[kh][:, md * 128:(md + 1) * 128],
                                        h2t[tt % 2][:, kh * 512:(kh + 1) * 512],
                                        start=(kh == 0), stop=(kh == HB - 1))
                s_pe.inc(ins, 1)
    b.block.tensor(ftensor)

    def fscalar(scalar):
        for tt in range(NYT):
            scalar.wait_ge(s_ob.h, 2 * tt + 1)
            if tt >= 1:
                scalar.wait_ge(s_ev.h, 2 * tt * KC)
            scalar.copy(obc2[:], pob2[:])
            scalar.drain()
            scalar.sem_inc(s_ob.h, 1)
            s_ob.n += 1
    b.block.scalar(fscalar)

    def fvector(vector):
        for tt in range(NYT):
            vector.wait_ge(s_ob.h, 2 * (tt + 1))
            for md in range(KC):
                n = tt * KC + md
                vector.wait_ge(s_pe.h, n + 1)
                vector.wait_ge(s_rd[n % 2].h, 16 * (n // 2 + 1))
                if n >= 2:
                    vector.wait_ge(s_st[n % 2].h, 16 * ((n - 2) // 2 + 1))
                vector.drain()
                vector.tensor_tensor(ysc2[n % 2][:], pyy[n % 2][:], obc2[:], ALU.mult)
                vector.drain()
                ins = vector.tensor_tensor(outsl[n % 2][:], ysc2[n % 2][:], rstg[n % 2][:],
                                           ALU.add)
                s_ev.inc(ins, 2)
    b.block.vector(fvector)
    barrier(b, bar, ("sync", "tensor", "scalar", "vector"))



def _moe_inputs(r1, Wg, W1, W2):
    maps = []
    for c in range(8):
        bidx, half = c // 2, c % 2
        rin = np.ascontiguousarray(r1[bidx, half * TOWN:(half + 1) * TOWN].T)
        maps.append({"rin": rin, "wg": np.ascontiguousarray(Wg),
                     "w1": np.ascontiguousarray(W1), "w2": np.ascontiguousarray(W2)})
    return maps


def _collect_rout(res):
    out = np.empty((4, TFULL, D), dtype=np.float32)
    for c in range(8):
        bidx, half = c // 2, c % 2
        out[bidx, half * TOWN:(half + 1) * TOWN] = np.asarray(res.results[c]["rout"]).T
    return out


def _bitffn_inputs(r3, w1, conv_w, conv_b, w2):
    w1T = np.ascontiguousarray(w1.T)
    cw_host = np.ascontiguousarray(
        conv_w[:, 0, :].reshape(HB, 128, 3).transpose(1, 0, 2)).reshape(128, HB * 3)
    cb_host = np.ascontiguousarray(conv_b.reshape(HB, 128).T)
    w2T = np.ascontiguousarray(w2.T)
    maps = []
    for c in range(8):
        bidx, half = c // 2, c % 2
        own = r3[bidx, half * TOWN:(half + 1) * TOWN]
        hl = (r3[bidx, half * TOWN - 1:half * TOWN] if half > 0
              else np.zeros((1, D), np.float32))
        hr = (r3[bidx, (half + 1) * TOWN:(half + 1) * TOWN + 1] if half == 0
              else np.zeros((1, D), np.float32))
        rin_ext = np.zeros((D, TEXT), np.float32)
        rin_ext[:, 0:TOWN] = own.T
        rin_ext[:, TOWN:TOWN + 1] = hl.T
        rin_ext[:, TOWN + 1:TOWN + 2] = hr.T
        maps.append({"rin": rin_ext, "w1T": w1T, "cw": cw_host, "cb": cb_host,
                     "w2T": w2T})
    return maps


def kernel(L0_attn_x, L0_attn_w_qkv, L0_attn_w_proj, L0_attn_b_proj,
           L0_ffn_Wg, L0_ffn_W1, L0_ffn_W2,
           L1_attn_W_qkv, L1_attn_b_qkv, L1_attn_W_out, L1_attn_b_out,
           L1_ffn_w1, L1_ffn_conv_w, L1_ffn_conv_b, L1_ffn_w2):
    f32 = np.float32
    x = np.asarray(L0_attn_x, f32)
    cores = list(range(8))

    nc_attn = build_attn_layer()
    nc_moe = build_moe_layer()
    nc_bit = build_bitffn_layer()

    wq, wk, wv = _canon_qkv_l0(np.asarray(L0_attn_w_qkv, f32))
    z = np.zeros(D, f32)
    maps = _attn_inputs(x, wq, wk, wv, z, z, z,
                        np.asarray(L0_attn_w_proj, f32), np.asarray(L0_attn_b_proj, f32))
    r1 = _attn_collect(run_bass_kernel_spmd(nc_attn, maps, core_ids=cores))

    maps = _moe_inputs(r1, np.asarray(L0_ffn_Wg, f32), np.asarray(L0_ffn_W1, f32),
                       np.asarray(L0_ffn_W2, f32))
    r2 = _collect_rout(run_bass_kernel_spmd(nc_moe, maps, core_ids=cores))

    wq, wk, wv = _canon_qkv_l1(np.asarray(L1_attn_W_qkv, f32))
    bq = np.asarray(L1_attn_b_qkv, f32)
    idx_q = np.concatenate([np.arange(h * 192, h * 192 + 64) for h in range(H)])
    maps = _attn_inputs(r2, wq, wk, wv, bq[idx_q], bq[idx_q + 64], bq[idx_q + 128],
                        np.asarray(L1_attn_W_out, f32), np.asarray(L1_attn_b_out, f32))
    r3 = _attn_collect(run_bass_kernel_spmd(nc_attn, maps, core_ids=cores))

    maps = _bitffn_inputs(r3, np.asarray(L1_ffn_w1, f32), np.asarray(L1_ffn_conv_w, f32),
                          np.asarray(L1_ffn_conv_b, f32), np.asarray(L1_ffn_w2, f32))
    r4 = _collect_rout(run_bass_kernel_spmd(nc_bit, maps, core_ids=cores))
    return r4
